# version 01 (final)
# baseline (speedup 1.0000x reference)
import contextlib
import numpy as np
import ml_dtypes
import concourse.bass as bass
import concourse.mybir as mybir
from concourse.bass_utils import run_bass_kernel_spmd

F32 = mybir.dt.float32
BF16 = mybir.dt.bfloat16
I32 = mybir.dt.int32
AF = mybir.ActivationFunctionType
OP = mybir.AluOpType
AP = bass.AP

D = 4096
DSSM = 8192
CONVD = 8448
PROJ = 16704
NH = 64
T = 2048
TH = 3
TT = T + TH
DC = 32
NKC = 16
CC = 64
XC = 66

TSL = [(0, 512), (512, 512), (1024, 512), (1536, 512), (2048, 3)]
TSL_T = [(0, 512), (512, 512), (1024, 512), (1536, 512)]

S0 = 147712
M0 = S0 + 45056


class Sem:
    def __init__(self, nc, stk, name):
        self.h = stk.enter_context(nc.semaphore(name))
        self.v = 0


def build_program(debug=(), stop=99, mode="full"):
    import os as _os
    stop = int(_os.environ.get("K_STOP", stop))
    stk = contextlib.ExitStack()
    nc = bass.Bass("TRN2", target_bir_lowering=False)

    def ein(name, shape, dt):
        return nc.dram_tensor(name, shape, dt, kind="ExternalInput").ap()

    x0T = ein("x0T", [D, TT], F32)
    win = ein("win", [D, PROJ], F32)
    wout = ein("wout", [DSSM, D], F32)
    wffnT = ein("wffnT", [D, D], F32)
    biasT = ein("biasT", [128, DC], F32)
    nwTi = ein("nwTi", [128, NH], F32)
    drepI = ein("drepI", [128, DSSM], BF16)
    dtbI = ein("dtbI", [NH, 1], F32)
    alogI = ein("alogI", [NH, 1], F32)
    convwI = ein("convwI", [128, XC * 4], F32)
    lATi = ein("lATi", [D, 48], F32)
    lBTi = ein("lBTi", [48, D], F32)
    lsiI = ein("lsiI", [4, 1], I32)
    metaI = ein("metaI", [4, 1], F32)
    rmaskI = ein("rmaskI", [128, 1], F32)
    cI128 = ein("cI128", [128, 128], BF16)
    cI128f = ein("cI128f", [128, 128], F32)
    cI64f = ein("cI64f", [64, 64], F32)
    cI64d = ein("cI64d", [64, 128], F32)
    cI4f = ein("cI4f", [4, 4], F32)
    cMask = ein("cMask", [128, 128], F32)
    cOnesB = ein("cOnesB", [128, 1], BF16)
    cOnesFr = ein("cOnesFr", [1, 128], F32)
    cREP = ein("cREP", [3, 48], F32)
    cIota3 = ein("cIota3", [1, 3], F32)

    xoutT = nc.dram_tensor("xoutT", [D, T], F32, kind="ExternalOutput").ap()

    bkind = {"full": "Internal", "A": "ExternalOutput", "B": "ExternalInput"}[mode]
    x1T = nc.dram_tensor("x1T", [D, TT], F32, kind=bkind).ap()
    x2T = nc.dram_tensor("x2T", [D, T], F32, kind="Internal").ap()
    zsp = nc.dram_tensor("zsp", [DSSM, T], BF16, kind=bkind).ap()
    xcv = nc.dram_tensor("xcv", [CONVD, T], BF16, kind=bkind).ap()
    ygd = nc.dram_tensor("ygd", [DSSM, T], BF16, kind="Internal").ap()
    cspl = nc.dram_tensor("cspl", [NH, T], F32, kind=bkind).ap()
    tspl = nc.dram_tensor("tspl", [128, 3072], F32, kind=bkind).ap()
    btspl = nc.dram_tensor("btspl", [128, NKC * 128], BF16, kind=bkind).ap()
    stown = nc.dram_tensor("stown", [128, DSSM], BF16,
                           kind="ExternalOutput" if mode == "A" else "Internal").ap()
    stg = nc.dram_tensor("stg", [256, DSSM], BF16, kind="Internal").ap()
    hinit = (nc.dram_tensor("hinit", [128, DSSM], BF16, kind="ExternalInput").ap()
             if mode == "B" else None)

    dbg = {}
    def dd(name, shape, dt):
        dbg[name] = nc.dram_tensor(name, shape, dt, kind="ExternalOutput").ap()

    if "u1" in debug: dd("d_u1", [128, DC * TT], BF16)
    if "bc1" in debug: dd("d_bcA", [128, TT], F32); dd("d_bcB", [128, TT], F32)
    if "lm" in debug:
        dd("d_m48", [48, 1], F32); dd("d_lrow", [1, 16], F32); dd("d_cmp4", [4, 1], F32)
    if "x1" in debug: dd("d_x1", [D, TT], F32)
    if "u2" in debug: dd("d_u2", [128, DC * TT], BF16)
    if "winout" in debug:
        dd("d_z", [DSSM, T], BF16); dd("d_xcv", [CONVD, T], BF16)
        dd("d_dtraw", [NH, TT], F32)
    if "scal" in debug:
        dd("d_c", [NH, T], F32); dd("d_tsp", [128, 3072], F32)
        dd("d_dt", [NH, TT], F32)
    if "pass1" in debug:
        dd("d_h", [128, DSSM], BF16); dd("d_hin", [128, DSSM], BF16)
    if "yg" in debug:
        dd("d_yg", [DSSM, T], BF16); dd("d_rstd", [1, T + 1024], F32)
    if "hp2" in debug: dd("d_hp2", [128, DSSM], BF16)
    if "ssdint" in debug:
        dd("d_g0m", [128, 128], F32); dd("d_dec", [128, 1024], BF16)
        dd("d_esc", [128, 1024], BF16); dd("d_gpp", [128, 1024], BF16)
        dd("d_dmin", [128, 1024], F32); dd("d_ysum", [128, 512], F32)
        dd("d_axt", [128, DSSM], BF16); dd("d_aszn", [128, DSSM], BF16)
        dd("d_ayg2", [128, DSSM], BF16); dd("d_axsw", [128, DSSM], BF16)
        dd("d_adi", [128, DSSM], BF16); dd("d_az", [128, DSSM], BF16)
        dd("d_sbt", [128, 2048], BF16); dd("d_sct", [128, 2048], BF16)
        dd("d_sbtok", [128, 2048], BF16); dd("d_dtw", [128, 1024], F32)
        dd("d_erep", [128, 1024], F32); dd("d_lndtc", [128, 1024], F32)
    if "x2" in debug: dd("d_x2", [D, T], F32)

    SB_BASE = 24576
    def sb(name, shape, dt, off):
        assert off % 32 == 0, name
        n = 1
        for s in shape[1:]:
            n *= s
        assert SB_BASE + off + n * mybir.dt.size(dt) <= 229247, name
        return nc.alloc_sbuf_tensor_at(name, shape, dt, offset=SB_BASE + off)

    uA = sb("uA", [128, DC * TT], BF16, 0)
    a_yg = sb("a_yg", [128, DSSM], BF16, 0)
    a_xt = [sb("a_xt0", [128, DSSM], BF16, 16384),
            sb("a_xt1", [128, DSSM], BF16, 32768)]
    a_xsw = sb("a_xsw", [128, DSSM], BF16, 49152)
    a_sq = sb("a_sq", [128, DSSM], BF16, 49152)
    a_di = sb("a_di", [128, DSSM], BF16, 65536)
    a_z = sb("a_z", [128, DSSM], BF16, 81920)
    a_szn = sb("a_szn", [128, DSSM], BF16, 98304)
    GRP = 114688
    g_dmin = sb("g_dmin", [128, 1024], F32, GRP)
    g_t1 = sb("g_t1", [128, 512], F32, GRP + 4096)
    g_ysum = sb("g_ysum", [128, 512], F32, GRP + 6144)
    g_esc = sb("g_esc", [128, 1024], BF16, GRP + 8192)
    g_dec = sb("g_dec", [128, 1024], BF16, GRP + 10240)
    g_gpp = sb("g_gpp", [128, 1024], BF16, GRP + 12288)
    g_g0m = sb("g_g0m", [128, 128], F32, GRP + 14336)
    a_yh = sb("a_yh", [128, NH * 1024], BF16, 0)

    harena = sb("harena", [128, DSSM], BF16, 131264)
    dtS = sb("dtS", [NH, TT], F32, 131264)
    w_sbM = sb("w_sbM", [128, T], BF16, 139488)
    lBTs = sb("lBTs", [48, D], BF16, 131264)
    lATs = sb("lATs", [128, DC * 48], BF16, 139456)
    f_rrmM = sb("f_rrmM", [48, TT], BF16, 142528)

    ln_x = [sb("ln_x0", [128, TT], F32, S0), sb("ln_x1", [128, TT], F32, S0 + 8224)]
    ln_bf = [sb("ln_bf0", [128, TT], BF16, S0 + 16448),
             sb("ln_bf1", [128, TT], BF16, S0 + 20576)]
    ln_sq = [sb("ln_sq0", [128, TT], BF16, S0 + 24704),
             sb("ln_sq1", [128, TT], BF16, S0 + 28832)]
    rowA = sb("rowA", [1, TT], F32, S0 + 24704)
    rowB = sb("rowB", [1, TT], F32, S0 + 32960)
    bcA = sb("bcA", [128, TT], F32, S0 + 16448)
    bcB = sb("bcB", [128, TT], F32, S0 + 32960)
    ln_tmp = sb("ln_tmp", [128, TT], F32, S0 + 24704)
    rowC = sb("rowC", [1, TT], F32, S0 + 16448)

    f_wb = [sb("f_wb0", [128, DC * 128], BF16, S0),
            sb("f_wb1", [128, DC * 128], BF16, S0 + 8192)]
    f_rrA = sb("f_rrA", [48, TT], F32, S0 + 16448)
    f_x = [sb("f_x0", [128, TT], F32, S0 + 16448),
           sb("f_x1", [128, TT], F32, S0 + 24704)]
    f_et = [sb("f_et0", [128, 512], F32, S0 + 32960),
            sb("f_et1", [128, 512], F32, S0 + 35008)]
    f_ob = [sb("f_ob0", [128, 512], F32, S0 + 37056),
            sb("f_ob1", [128, 512], F32, S0 + 39104)]

    w_wb = [sb("w_wb0", [128, DC * 128], BF16, S0),
            sb("w_wb1", [128, DC * 128], BF16, S0 + 8192)]
    w_cb = [sb("w_cb0", [128, TT], F32, S0 + 16448),
            sb("w_cb1", [128, TT], F32, S0 + 24704)]
    zbuf = [sb("zbuf0", [128, TT], BF16, S0 + 16448),
            sb("zbuf1", [128, TT], BF16, S0 + 24704)]
    w_ev = sb("w_ev", [128, T], F32, S0 + 32960)

    cS = sb("cS", [NH, T], F32, S0)
    lnSb = sb("lnSb", [NH, T], F32, S0 + 33536)
    sS = sb("sS", [NH, T], F32, S0 + 16448)
    rmS = sb("rmS", [NH, T], F32, S0 + 24704)
    wtmp = sb("wtmp", [NH, 128], F32, S0 + 32960)
    dttmp = sb("dttmp", [NH, T], F32, S0 + 33536)

    sBT = sb("sBT", [128, NKC * 128], BF16, S0)
    sCT = sb("sCT", [128, NKC * 128], BF16, S0 + 4096)
    sBtok = sb("sBtok", [128, NKC * 128], BF16, S0 + 8192)
    dtwT = sb("dtwT", [128, NKC * NH], F32, S0 + 12288)
    erepT = sb("erepT", [128, NKC * NH], F32, S0 + 16384)
    lndtcT = sb("lndtcT", [128, NKC * NH], F32, S0 + 20480)
    flc = [sb("flc0", [1, 2048], F32, S0 + 24576),
           sb("flc1", [1, 2048], F32, S0 + 32768)]
    rbB = [sb("rbB0", [128, 512], F32, S0 + 40960),
           sb("rbB1", [128, 512], F32, S0 + 43008)]
    rstdH = sb("rstdH", [1, 1024], F32, S0 + 40960)

    wo_b = [sb("wo_b0", [128, CC * 128], BF16, S0),
            sb("wo_b1", [128, CC * 128], BF16, S0 + 16384)]
    wo_x = [sb("wo_x0", [128, 1024], F32, S0 + 32768),
            sb("wo_x1", [128, 1024], F32, S0 + 36864)]

    mcur = [M0]
    def msb(name, shape, dt):
        n = 1
        for s in shape[1:]:
            n *= s
        n *= mybir.dt.size(dt)
        t = sb(name, shape, dt, mcur[0])
        mcur[0] += (n + 31) // 32 * 32
        return t

    convwS = msb("convwS", [128, XC * 4], F32)
    biasS = msb("biasS", [128, DC], F32)
    nwS = msb("nwS", [128, NH], F32)
    dtbS = msb("dtbS", [NH, 1], F32)
    alogS = msb("alogS", [NH, 1], F32)
    anS = msb("anS", [NH, 1], F32)
    rmaskS = msb("rmaskS", [128, 1], F32)
    i128S = msb("i128S", [128, 128], BF16)
    i64fS = msb("i64fS", [64, 64], F32)
    i64dS = msb("i64dS", [64, 128], F32)
    i4fS = msb("i4fS", [4, 4], F32)
    maskS = msb("maskS", [128, 128], F32)
    onesBS = msb("onesBS", [128, 1], BF16)
    onesFrS = msb("onesFrS", [1, 128], F32)
    repS = msb("repS", [3, 48], F32)
    iota3S = msb("iota3S", [1, 3], F32)
    lsiS = msb("lsiS", [4, 1], I32)
    lsifS = msb("lsifS", [4, 1], F32)
    metaS = msb("metaS", [4, 1], F32)
    cmp4S = msb("cmp4S", [4, 1], F32)
    m48S = msb("m48S", [48, 1], F32)
    lrowS = msb("lrowS", [1, 16], F32)
    cendS = msb("cendS", [NH, 16], F32)
    eS = msb("eS", [NH, 16], F32)
    edS = msb("edS", [NH, 16], F32)
    wo_e = [msb("wo_e0", [128, 512], F32), msb("wo_e1", [128, 512], F32)]
    sqsR = msb("sqsR", [1, 128], F32)
    epsS = msb("epsS", [1, 1], F32)
    assert SB_BASE + mcur[0] <= 229247, f"SBUF smalls overflow: {mcur[0]}"

    sems = {}
    for name in ["LD", "LDB", "LDZ", "LDC", "LG", "LGB", "LGC",
                 "ST", "STB", "STO", "FLA", "FLB",
                 "PE", "AC", "DV", "PL", "EV", "EW", "CC", "SV", "SA"]:
        sems[name] = Sem(nc, stk, "s" + name)
    (LD, LDB, LDZ, LDC, LG, LGB, LGC, ST, STB, STO, FLA, FLB,
     PE, ACs, DV, PL, EV, EW, CCs, SV, SA) = (sems[k] for k in
        ["LD", "LDB", "LDZ", "LDC", "LG", "LGB", "LGC",
         "ST", "STB", "STO", "FLA", "FLB",
         "PE", "AC", "DV", "PL", "EV", "EW", "CC", "SV", "SA"])
    LDp = [LD, LDB]
    LGp = [LG, LGB]
    STp = [ST, STB]
    FLp = [FLA, FLB]

    def vs(v, inst):
        inst.then_inc(SV.h, 1)
        SV.v += 1
        v.wait_ge(SV.h, SV.v)
        return inst

    def asy(sc, inst):
        inst.then_inc(SA.h, 1)
        SA.v += 1
        sc.wait_ge(SA.h, SA.v)
        return inst

    with nc.Block("prolog") as blk:
        b_ld, b_lg = LD.v, LG.v

        @blk.sync
        def _(s):
            for dst, src in [(convwS, convwI), (biasS, biasT), (nwS, nwTi),
                             (dtbS, dtbI), (alogS, alogI), (rmaskS, rmaskI),
                             (i128S, cI128), (i64fS, cI64f),
                             (i4fS, cI4f), (i64dS, cI64d), (maskS, cMask), (onesBS, cOnesB),
                             (onesFrS, cOnesFr), (repS, cREP), (iota3S, cIota3),
                             (lsiS, lsiI), (metaS, metaI)]:
                s.dma_start(out=dst[:, :], in_=src).then_inc(LD.h, 16)
            LD.v += 16 * 17

        @blk.vector
        def _(v):
            v.wait_ge(LD.h, b_ld + 16 * 17)
            v.memset(epsS[:, :], 1e-5)
            v.tensor_scalar_mul(biasS[:, :], biasS[:, :], 0.5)
            vs(v, v.tensor_copy(lsifS[:, :], lsiS[:, :]))
            vs(v, v.tensor_scalar(cmp4S[:, :], lsifS[:, :], metaS[:, :], None, OP.is_gt))
            v.engine_nop().then_inc(DV.h, 1)
            DV.v += 1

    with nc.psum_tensor("pp0", [128, 128], F32) as pp0, nc.Block("loramask") as blk:
        b_pe, b_dv = PE.v, DV.v

        @blk.tensor
        def _(t):
            t.transpose(pp0[0:1, 0:4], cmp4S[:, :], i4fS[:, :]).then_inc(PE.h, 1)
            t.wait_ge(DV.h, b_dv + 1)
            t.transpose(pp0[0:3, 4:5], lrowS[0:1, 8:11], i4fS[0:1, 0:1]).then_inc(PE.h, 1)
            t.wait_ge(DV.h, b_dv + 2)
            t.matmul(pp0[0:48, 8:9], repS[:, :], cmp4S[0:3, 0:1],
                     start=True, stop=True).then_inc(PE.h, 1)
            PE.v += 3

        @blk.vector
        def _(v):
            v.wait_ge(PE.h, b_pe + 1)
            cr = lrowS
            vs(v, v.tensor_copy(cr[0:1, 0:4], pp0[0:1, 0:4]))
            c0, c1, c2, c3 = (cr[0:1, i:i + 1] for i in range(4))
            t5, t6, t7 = cr[0:1, 5:6], cr[0:1, 6:7], cr[0:1, 7:8]
            vs(v, v.tensor_tensor(t5, c1, c0, OP.mult))
            vs(v, v.tensor_tensor(t5, c1, t5, OP.add))
            vs(v, v.tensor_scalar(t5, t5, -1.0, 2.0, OP.mult, OP.add))
            vs(v, v.tensor_tensor(t5, t5, c2, OP.mult))
            vs(v, v.tensor_scalar(t6, c3, -1.0, 4.0, OP.mult, OP.add))
            vs(v, v.tensor_scalar(t7, c2, -1.0, 1.0, OP.mult, OP.add))
            vs(v, v.tensor_tensor(t6, t6, t7, OP.mult))
            vs(v, v.tensor_tensor(t5, t5, t6, OP.add))
            vs(v, v.tensor_scalar(t6, t5, 1.0, None, OP.subtract))
            vs(v, v.tensor_scalar(t7, t6, 1.0, None, OP.is_ge))
            vs(v, v.tensor_scalar(t5, t6, 1.0, 0.0, OP.subtract, OP.max))
            vs(v, v.tensor_scalar(t5, t5, 2.0, None, OP.min))
            m3 = cr[0:1, 8:11]
            vs(v, v.tensor_scalar(m3, iota3S[:, :], t5, None, OP.is_equal))
            vs(v, v.tensor_tensor(m3, m3,
                            AP(t7.tensor, t7.offset, [[16, 1], [0, 3]]), OP.mult))
            v.engine_nop().then_inc(DV.h, 1)
            v.wait_ge(PE.h, b_pe + 2)
            v.tensor_copy(cmp4S[0:3, 0:1], pp0[0:3, 4:5])
            v.engine_nop().then_inc(DV.h, 1)
            v.wait_ge(PE.h, b_pe + 3)
            v.tensor_copy(m48S[:, :], pp0[0:48, 8:9])
            v.engine_nop().then_inc(DV.h, 1)
            DV.v += 3

    def emit_ln(tag, src, width, slices, dst_stride):
        ns = len(slices)
        with (
            nc.psum_tensor("pstA" + tag, [96, 512], F32) as pstA,
            nc.psum_tensor("pstB" + tag, [96, 512], F32) as pstB,
            nc.psum_tensor("pstC" + tag, [96, 512], F32) as pstC,
            nc.psum_tensor("pstD" + tag, [96, 512], F32) as pstD,
            nc.psum_tensor("pbc0" + tag, [128, 512], F32) as pbc0,
            nc.psum_tensor("pbc1" + tag, [128, 512], F32) as pbc1,
            nc.Block("ln" + tag) as blk,
        ):
            regs = []
            for bank in (pstA, pstB, pstC, pstD):
                for base in (0, 32, 64):
                    regs.append((bank, base))
            b_ld = [LD.v, LDB.v]
            b_dv, b_ac, b_pe, b_ev = DV.v, ACs.v, PE.v, EV.v

            @blk.sync
            def _(s):
                for i in range(DC):
                    if i >= 2:
                        s.wait_ge(DV.h, b_dv + i - 1)
                    s.dma_start(out=ln_x[i % 2][:, 0:width],
                                in_=src[i * 128:(i + 1) * 128, 0:width]
                                ).then_inc(LDp[i % 2].h, 16)
                LD.v += 16 * (DC // 2)
                LDB.v += 16 * (DC // 2)

            @blk.vector
            def _(v):
                for i in range(DC):
                    v.wait_ge(LDp[i % 2].h, b_ld[i % 2] + 16 * (i // 2 + 1))
                    if i >= 2:
                        v.wait_ge(PE.h, b_pe + i - 1)
                    v.tensor_copy(ln_bf[i % 2][:, 0:width], ln_x[i % 2][:, 0:width])
                    v.engine_nop().then_inc(DV.h, 1)
                DV.v += DC

            @blk.scalar
            def _(sc):
                for i in range(DC):
                    sc.wait_ge(DV.h, b_dv + i + 1)
                    if i >= 2:
                        sc.wait_ge(PE.h, b_pe + i - 1)
                    sc.activation(ln_sq[i % 2][:, 0:width], ln_bf[i % 2][:, 0:width],
                                  AF.Square).then_inc(ACs.h, 1)
                ACs.v += DC

            @blk.tensor
            def _(t):
                for i in range(DC):
                    t.wait_ge(ACs.h, b_ac + i + 1)
                    inst = None
                    for j, (o, w) in enumerate(slices):
                        bank, base = regs[j]
                        inst = t.matmul(bank[base:base + 1, 0:w], onesBS[:, :],
                                        ln_bf[i % 2][:, o:o + w],
                                        start=(i == 0), stop=(i == DC - 1),
                                        skip_group_check=True)
                    for j, (o, w) in enumerate(slices):
                        bank, base = regs[ns + j]
                        inst = t.matmul(bank[base:base + 1, 0:w], onesBS[:, :],
                                        ln_sq[i % 2][:, o:o + w],
                                        start=(i == 0), stop=(i == DC - 1),
                                        skip_group_check=True)
                    inst.then_inc(PE.h, 1)
                PE.v += DC

            @blk.vector
            def _(v):
                v.wait_ge(PE.h, b_pe + DC)
                for j, (o, w) in enumerate(slices):
                    bank, base = regs[j]
                    vs(v, v.tensor_copy(rowA[0:1, o:o + w], bank[base:base + 1, 0:w]))
                for j, (o, w) in enumerate(slices):
                    bank, base = regs[ns + j]
                    vs(v, v.tensor_copy(rowB[0:1, o:o + w], bank[base:base + 1, 0:w]))
                vs(v, v.tensor_scalar_mul(rowA[0:1, 0:width], rowA[0:1, 0:width], 1.0 / D))
                vs(v, v.tensor_scalar_mul(rowB[0:1, 0:width], rowB[0:1, 0:width], 1.0 / D))
                vs(v, v.tensor_tensor(rowC[0:1, 0:width], rowA[0:1, 0:width],
                                rowA[0:1, 0:width], OP.mult))
                vs(v, v.tensor_tensor(rowB[0:1, 0:width], rowB[0:1, 0:width],
                                rowC[0:1, 0:width], OP.subtract))
                v.engine_nop().then_inc(DV.h, 1)
                DV.v += 1

            @blk.scalar
            def _(sc):
                sc.wait_ge(DV.h, b_dv + DC + 1)
                sc.activation(rowB[0:1, 0:width], rowB[0:1, 0:width], AF.Sqrt,
                              bias=epsS[0:1, 0:1]).then_inc(ACs.h, 1)
                ACs.v += 1

            @blk.vector
            def _(v):
                v.wait_ge(ACs.h, b_ac + DC + 1)
                vs(v, v.reciprocal(rowB[0:1, 0:width], rowB[0:1, 0:width]))
                vs(v, v.tensor_tensor(rowA[0:1, 0:width], rowA[0:1, 0:width],
                                rowB[0:1, 0:width], OP.mult))
                v.engine_nop().then_inc(DV.h, 1)
                DV.v += 1

            @blk.tensor
            def _(t):
                t.wait_ge(DV.h, b_dv + DC + 2)
                for j, (o, w) in enumerate(slices):
                    if j >= 1:
                        t.wait_ge(EV.h, b_ev + j)
                    t.matmul(pbc0[:, 0:w], onesFrS[:, :], rowB[0:1, o:o + w],
                             start=True, stop=True, skip_group_check=True)
                    t.matmul(pbc1[:, 0:w], onesFrS[:, :], rowA[0:1, o:o + w],
                             start=True, stop=True,
                             skip_group_check=True).then_inc(PE.h, 1)
                PE.v += ns

            @blk.vector
            def _(v):
                for j, (o, w) in enumerate(slices):
                    v.wait_ge(PE.h, b_pe + DC + j + 1)
                    v.tensor_copy(bcA[:, o:o + w], pbc0[:, 0:w])
                    v.tensor_copy(bcB[:, o:o + w], pbc1[:, 0:w])
                    v.engine_nop().then_inc(EV.h, 1)
                EV.v += ns

            b_lg2 = [LG.v, LGB.v]
            b_dv2 = DV.v

            @blk.gpsimd
            def _(g):
                for i in range(DC):
                    if i >= 2:
                        g.wait_ge(DV.h, b_dv2 + 2 * (i - 1))
                    else:
                        g.wait_ge(EV.h, b_ev + ns)
                    g.dma_start(out=ln_x[i % 2][:, 0:width],
                                in_=src[i * 128:(i + 1) * 128, 0:width]
                                ).then_inc(LGp[i % 2].h, 16)
                LG.v += 16 * (DC // 2)
                LGB.v += 16 * (DC // 2)

            @blk.vector
            def _(v):
                for i in range(DC):
                    v.wait_ge(LGp[i % 2].h, b_lg2[i % 2] + 16 * (i // 2 + 1))
                    vs(v, v.tensor_tensor(ln_tmp[:, 0:width], ln_x[i % 2][:, 0:width],
                                    bcA[:, 0:width], OP.mult))
                    v.engine_nop().then_inc(DV.h, 1)
                    v.tensor_tensor(uA[:, i * dst_stride:i * dst_stride + width],
                                    ln_tmp[:, 0:width], bcB[:, 0:width],
                                    OP.subtract).then_inc(DV.h, 1)
                    v.wait_ge(DV.h, b_dv2 + 2 * i + 2)
                DV.v += 2 * DC

    def emit_ffn(tag, width, slices, resid_src, dst, u_stride):
        ns = len(slices)
        with (
            nc.psum_tensor("pm0" + tag, [128, 512], F32) as pm0,
            nc.psum_tensor("pm1" + tag, [128, 512], F32) as pm1,
            nc.psum_tensor("pm2" + tag, [128, 512], F32) as pm2,
            nc.psum_tensor("prr0" + tag, [48, 512], F32) as prr0,
            nc.psum_tensor("prr1" + tag, [48, 512], F32) as prr1,
            nc.Block("ffn" + tag) as blk,
        ):
            pms = [pm0, pm1, pm2]
            prrs = [prr0, prr1]
            b_pe, b_dv, b_ac, b_lgc = PE.v, DV.v, ACs.v, LGC.v
            b_lg = [LG.v, LGB.v]
            b_ld = [LD.v, LDB.v]
            b_st = [ST.v, STB.v]

            @blk.gpsimd
            def _(g):
                g.dma_start(out=lATs[:, :],
                            in_=AP(lATi.tensor, 0,
                                   [[48, 128], [128 * 48, DC], [1, 48]])
                            ).then_inc(LGC.h, 16)
                g.dma_start(out=lBTs[:, :], in_=lBTi).then_inc(LGC.h, 16)
                LGC.v += 32

            @blk.tensor
            def _(t):
                t.wait_ge(LGC.h, b_lgc + 32)
                for j, (o, w) in enumerate(slices):
                    if j >= 2:
                        t.wait_ge(DV.h, b_dv + j - 1)
                    inst = None
                    for dc in range(DC):
                        inst = t.matmul(prrs[j % 2][0:48, 0:w],
                                        lATs[:, dc * 48:(dc + 1) * 48],
                                        uA[:, dc * u_stride + o:dc * u_stride + o + w],
                                        start=(dc == 0), stop=(dc == DC - 1))
                    inst.then_inc(PE.h, 1)
                PE.v += ns

            @blk.vector
            def _(v):
                for j, (o, w) in enumerate(slices):
                    v.wait_ge(PE.h, b_pe + j + 1)
                    vs(v, v.tensor_copy(f_rrA[0:48, o:o + w], prrs[j % 2][0:48, 0:w]))
                    v.engine_nop().then_inc(DV.h, 1)
                vs(v, v.tensor_scalar(f_rrmM[0:48, 0:width], f_rrA[0:48, 0:width],
                                m48S[0:48, 0:1], None, OP.mult))
                v.engine_nop().then_inc(DV.h, 1)
                DV.v += ns + 1
            b_dv2 = DV.v
            b_pe2 = PE.v

            @blk.gpsimd
            def _(g):
                g.wait_ge(DV.h, b_dv2)
                for ot in range(DC):
                    if ot >= 2:
                        g.wait_ge(PE.h, b_pe2 + ns * (ot - 1))
                    g.dma_start(out=f_wb[ot % 2][:, :],
                                in_=AP(wffnT.tensor, ot * 128,
                                       [[D, 128], [128 * D, DC], [1, 128]])
                                ).then_inc(LGp[ot % 2].h, 16)
                LG.v += 16 * (DC // 2)
                LGB.v += 16 * (DC // 2)

            @blk.sync
            def _(s):
                s.wait_ge(DV.h, b_dv2)
                for ot in range(DC):
                    if ot >= 2:
                        s.wait_ge(DV.h, b_dv2 + ns * (ot - 1))
                    s.dma_start(out=f_x[ot % 2][:, 0:width],
                                in_=resid_src[ot * 128:(ot + 1) * 128, 0:width]
                                ).then_inc(LDp[ot % 2].h, 16)
                LD.v += 16 * (DC // 2)
                LDB.v += 16 * (DC // 2)

            @blk.tensor
            def _(t):
                for ot in range(DC):
                    t.wait_ge(LGp[ot % 2].h, b_lg[ot % 2] + 16 * (ot // 2 + 1))
                    for j, (o, w) in enumerate(slices):
                        idx = ot * ns + j
                        if idx >= 3:
                            t.wait_ge(ACs.h, b_ac + idx - 2)
                        bank = pms[idx % 3]
                        for dc in range(DC):
                            t.matmul(bank[:, 0:w],
                                     f_wb[ot % 2][:, dc * 128:(dc + 1) * 128],
                                     uA[:, dc * u_stride + o:dc * u_stride + o + w],
                                     start=(dc == 0), stop=False)
                        t.matmul(bank[:, 0:w], lBTs[0:48, ot * 128:(ot + 1) * 128],
                                 f_rrmM[0:48, o:o + w],
                                 start=False, stop=True).then_inc(PE.h, 1)
                PE.v += DC * ns

            @blk.scalar
            def _(sc):
                for ot in range(DC):
                    for j, (o, w) in enumerate(slices):
                        idx = ot * ns + j
                        sc.wait_ge(PE.h, b_pe2 + idx + 1)
                        if idx >= 2:
                            sc.wait_ge(DV.h, b_dv2 + idx - 1)
                        sc.activation(f_et[idx % 2][:, 0:w], pms[idx % 3][:, 0:w],
                                      AF.Identity, bias=biasS[:, ot:ot + 1],
                                      scale=0.5).then_inc(ACs.h, 1)
                        if idx >= 1:
                            pidx = idx - 1
                            pot, pj = divmod(pidx, ns)
                            po, pw = slices[pj]
                            sc.wait_ge(DV.h, b_dv2 + pidx + 1)
                            sc.dma_start(out=dst[pot * 128:(pot + 1) * 128, po:po + pw],
                                         in_=f_ob[pidx % 2][:, 0:pw]
                                         ).then_inc(STp[pidx % 2].h, 16)
                pidx = DC * ns - 1
                pot, pj = divmod(pidx, ns)
                po, pw = slices[pj]
                sc.wait_ge(DV.h, b_dv2 + pidx + 1)
                sc.dma_start(out=dst[pot * 128:(pot + 1) * 128, po:po + pw],
                             in_=f_ob[pidx % 2][:, 0:pw]).then_inc(STp[pidx % 2].h, 16)
                ACs.v += DC * ns
                ST.v += 16 * ((DC * ns + 1) // 2)
                STB.v += 16 * (DC * ns // 2)

            @blk.vector
            def _(v):
                for ot in range(DC):
                    v.wait_ge(LDp[ot % 2].h, b_ld[ot % 2] + 16 * (ot // 2 + 1))
                    for j, (o, w) in enumerate(slices):
                        idx = ot * ns + j
                        v.wait_ge(ACs.h, b_ac + idx + 1)
                        if idx >= 2:
                            v.wait_ge(STp[idx % 2].h, b_st[idx % 2] + 16 * (idx // 2))
                        v.tensor_tensor(f_ob[idx % 2][:, 0:w], f_et[idx % 2][:, 0:w],
                                        f_x[ot % 2][:, o:o + w],
                                        OP.add).then_inc(DV.h, 1)
                DV.v += DC * ns

            @blk.sync
            def _(s):
                s.wait_ge(ST.h, b_st[0] + 16 * ((DC * ns + 1) // 2))
                s.wait_ge(STB.h, b_st[1] + 16 * (DC * ns // 2))

    def emit_dbg_copy(name, dsts):
        with nc.Block("dbg" + name) as blk:
            @blk.sync
            def _(s):
                for dst, src in dsts:
                    s.dma_start(out=dst, in_=src).then_inc(STO.h, 16)
                    STO.v += 16
                s.wait_ge(STO.h, STO.v)

    def fin():
        stk.close()
        return nc, dbg
    if mode == "B":
        with nc.Block("hinitload") as blk:
            @blk.sync
            def _(s):
                s.dma_start(out=harena[:, :], in_=hinit).then_inc(LD.h, 16)
                LD.v += 16
                s.wait_ge(LD.h, LD.v)
    if mode != "B":
        if "lm" in debug:
            emit_dbg_copy("lm", [(dbg["d_m48"], m48S[:, :]), (dbg["d_lrow"], lrowS[:, :]),
                                 (dbg["d_cmp4"], cmp4S[:, :])])
        emit_ln("1", x0T, TT, TSL, TT)
        if "bc1" in debug:
            emit_dbg_copy("bc1", [(dbg["d_bcA"], bcA[:, :]), (dbg["d_bcB"], bcB[:, :])])
        if "u1" in debug:
            emit_dbg_copy("u1", [(dbg["d_u1"], uA[:, :])])
        if stop <= 1: return fin()
        emit_ffn("1", TT, TSL, x0T, x1T, TT)
        if "x1" in debug:
            emit_dbg_copy("x1", [(dbg["d_x1"], x1T)])
        if stop <= 2: return fin()
        emit_ln("2", x1T, TT, TSL, TT)
        if "u2" in debug:
            emit_dbg_copy("u2", [(dbg["d_u2"], uA[:, :])])
        if stop <= 3: return fin()

    if mode != "B":
        NCT = 131
        ctw = [128] * 130 + [64]
        with (
            nc.psum_tensor("pw0", [128, 512], F32) as pw0,
            nc.psum_tensor("pw1", [128, 512], F32) as pw1,
            nc.psum_tensor("pw2", [128, 512], F32) as pw2,
            nc.Block("win") as blk,
        ):
            pws = [pw0, pw1, pw2]
            ns = len(TSL)
            b_pe, b_dv, b_ac, b_ev = PE.v, DV.v, ACs.v, EV.v
            b_lg = [LG.v, LGB.v]
            b_st = [ST.v, STB.v]

            @blk.gpsimd
            def _(g):
                for ct in range(NCT):
                    w = ctw[ct]
                    if ct >= 2:
                        g.wait_ge(PE.h, b_pe + ns * (ct - 1))
                    g.dma_start(out=w_wb[ct % 2][:, 0:DC * w],
                                in_=AP(win.tensor, ct * 128,
                                       [[PROJ, 128], [128 * PROJ, DC], [1, w]])
                                ).then_inc(LGp[ct % 2].h, 16)
                LG.v += 16 * ((NCT + 1) // 2)
                LGB.v += 16 * (NCT // 2)

            @blk.tensor
            def _(t):
                for ct in range(NCT):
                    w = ctw[ct]
                    t.wait_ge(LGp[ct % 2].h, b_lg[ct % 2] + 16 * (ct // 2 + 1))
                    for j, (o, sw) in enumerate(TSL):
                        idx = ct * ns + j
                        if idx >= 3:
                            t.wait_ge(EV.h, b_ev + idx - 2)
                        bank = pws[idx % 3]
                        inst = None
                        for dc in range(DC):
                            inst = t.matmul(bank[0:w, 0:sw],
                                            w_wb[ct % 2][:, dc * w:(dc + 1) * w],
                                            uA[:, dc * TT + o:dc * TT + o + sw],
                                            start=(dc == 0), stop=(dc == DC - 1))
                        inst.then_inc(PE.h, 1)
                PE.v += NCT * ns

            @blk.vector
            def _(v):
                ndv = 0
                for ct in range(NCT):
                    w = ctw[ct]
                    if ct >= 2 and ct < 130:
                        v.wait_ge(STp[ct % 2].h, b_st[ct % 2] + 16 * (ct // 2))
                    for j, (o, sw) in enumerate(TSL):
                        idx = ct * ns + j
                        v.wait_ge(PE.h, b_pe + idx + 1)
                        bank = pws[idx % 3]
                        if ct < 64:
                            v.tensor_copy(zbuf[ct % 2][:, o:o + sw], bank[:, 0:sw])
                        elif ct < 130:
                            v.tensor_copy(w_cb[ct % 2][:, o:o + sw], bank[:, 0:sw])
                        else:
                            v.tensor_copy(dtS[0:64, o:o + sw], bank[0:64, 0:sw])
                        v.engine_nop().then_inc(EV.h, 1)
                    if 64 <= ct < 130:
                        cx = ct - 64
                        if cx >= 1:
                            v.wait_ge(ACs.h, b_ac + cx)
                        vs(v, v.tensor_scalar(w_cb[ct % 2][:, 0:TH], w_cb[ct % 2][:, 0:TH],
                                        rmaskS[:, :], None, OP.mult))
                        vs(v, v.tensor_scalar(w_ev[:, 0:T], w_cb[ct % 2][:, 0:T],
                                        convwS[:, 4 * cx:4 * cx + 1], None, OP.mult))
                        for k in range(1, 4):
                            vs(v, v.scalar_tensor_tensor(w_ev[:, 0:T],
                                                   w_cb[ct % 2][:, k:k + T],
                                                   convwS[:, 4 * cx + k:4 * cx + k + 1],
                                                   w_ev[:, 0:T], OP.mult, OP.add))
                        v.engine_nop().then_inc(DV.h, 1)
                        ndv += 1
                EV.v += NCT * ns
                DV.v += ndv

            @blk.scalar
            def _(sc):
                for ct in range(130):
                    if ct < 64:
                        sc.wait_ge(EV.h, b_ev + (ct + 1) * ns)
                        sc.dma_start(out=zsp[ct * 128:(ct + 1) * 128, 0:T],
                                     in_=zbuf[ct % 2][:, TH:TT]).then_inc(STp[ct % 2].h, 16)
                    else:
                        cx = ct - 64
                        sc.wait_ge(DV.h, b_dv + cx + 1)
                        if cx >= 1:
                            pct = ct - 1
                            sc.wait_ge(STp[pct % 2].h,
                                       b_st[pct % 2] + 16 * (pct // 2 + 1))
                        sc.activation(w_sbM[:, 0:T], w_ev[:, 0:T], AF.Silu).then_inc(ACs.h, 1)
                        sc.wait_ge(ACs.h, b_ac + cx + 1)
                        sc.dma_start(out=xcv[cx * 128:(cx + 1) * 128, 0:T],
                                     in_=w_sbM[:, 0:T]).then_inc(STp[ct % 2].h, 16)
                ST.v += 16 * 65
                STB.v += 16 * 65
                ACs.v += 66

            @blk.sync
            def _(s):
                s.wait_ge(ST.h, b_st[0] + 16 * 65)
                s.wait_ge(STB.h, b_st[1] + 16 * 65)
                s.wait_ge(EV.h, b_ev + NCT * ns)

        if "winout" in debug:
            emit_dbg_copy("win", [(dbg["d_z"], zsp), (dbg["d_xcv"], xcv),
                                  (dbg["d_dtraw"], dtS[:, :])])
        if stop <= 4: return fin()

        import os as _os
        S5 = int(_os.environ.get("K_S5PART", "7"))
        with (
            nc.psum_tensor("pt4", [128, 128], F32) as pt4,
            nc.psum_tensor("pt5", [128, 128], F32) as pt5,
            nc.Block("scal") as blk,
        ):
            b_dv, b_ac, b_pe, b_st, b_ev = DV.v, ACs.v, PE.v, ST.v, EV.v

            @blk.vector
            def _(v):
                dr = dtS[0:64, TH:TT]
                vs(v, v.tensor_scalar(dttmp[0:64, 0:T], dr, dtbS[:, :], None, OP.add))
                v.engine_nop().then_inc(DV.h, 1)
                DV.v += 1

            @blk.scalar
            def _(sc):
                sc.wait_ge(DV.h, b_dv + 1)
                asy(sc, sc.activation(sS[0:64, 0:T], dttmp[0:64, 0:T], AF.Abs))
                asy(sc, sc.activation(sS[0:64, 0:T], sS[0:64, 0:T], AF.Exp, scale=-1.0))
                sc.activation(sS[0:64, 0:T], sS[0:64, 0:T], AF.Ln,
                              bias=1.0).then_inc(ACs.h, 1)
                ACs.v += 1

            @blk.vector
            def _(v):
                v.wait_ge(ACs.h, b_ac + 1)
                vs(v, v.tensor_scalar(cS[0:64, 0:T], dttmp[0:64, 0:T], 0.0, None, OP.max))
                vs(v, v.tensor_tensor(dtS[0:64, 0:T], cS[0:64, 0:T], sS[0:64, 0:T], OP.add))
                v.engine_nop().then_inc(DV.h, 1)
                DV.v += 1

            @blk.scalar
            def _(sc):
                sc.wait_ge(DV.h, b_dv + 2)
                sc.activation(anS[:, :], alogS[:, :], AF.Exp)
                sc.activation(lnSb[0:64, 0:T], dtS[0:64, 0:T],
                              AF.Ln).then_inc(ACs.h, 1)
                ACs.v += 1

            @blk.vector
            def _(v):
                if S5 < 2:
                    v.engine_nop().then_inc(DV.h, 1)
                    DV.v += 1
                    return
                v.wait_ge(ACs.h, b_ac + 2)
                vs(v, v.tensor_scalar_mul(anS[:, :], anS[:, :], -1.0))
                vs(v, v.tensor_scalar(sS[0:64, 0:T], dtS[0:64, 0:T], anS[:, :], None, OP.mult))
                vs(v, v.memset(rmS[0:64, 0:T], 1.0))
                vs(v, v.memset(AP(rmS, 0, [[T, 64], [128, 16]]), 0.0))
                vs(v, v.tensor_tensor_scan(cS[0:64, 0:T], rmS[0:64, 0:T], sS[0:64, 0:T],
                                     0.0, OP.mult, OP.add))
                vs(v, v.tensor_scalar(lnSb[0:64, 0:T], lnSb[0:64, 0:T], -1e30, None, OP.max))
                vs(v, v.tensor_tensor(lnSb[0:64, 0:T], lnSb[0:64, 0:T], cS[0:64, 0:T],
                                OP.subtract))
                vs(v, v.tensor_copy(cendS[0:64, 0:16], AP(cS, 127, [[T, 64], [128, 16]])))
                v.engine_nop().then_inc(DV.h, 1)
                DV.v += 1

            @blk.scalar
            def _(sc):
                if S5 < 3:
                    sc.nop().then_inc(ACs.h, 1 + NKC)
                    ACs.v += 1 + NKC
                    return
                sc.wait_ge(DV.h, b_dv + 3)
                sc.activation(eS[0:64, 0:16], cendS[0:64, 0:16], AF.Exp).then_inc(ACs.h, 1)
                for k in range(NKC):
                    if k >= 1 and S5 >= 4:
                        sc.wait_ge(PE.h, b_pe + k)
                    sc.activation(wtmp[0:64, 0:128], cS[0:64, k * 128:(k + 1) * 128],
                                  AF.Exp, scale=-1.0,
                                  bias=cendS[0:64, k:k + 1]).then_inc(ACs.h, 1)
                ACs.v += 1 + NKC

            @blk.vector
            def _(v):
                if S5 < 3:
                    v.engine_nop().then_inc(DV.h, NKC)
                    v.engine_nop().then_inc(EV.h, 2 * NKC + NH)
                    DV.v += NKC
                    EV.v += 2 * NKC + NH
                    return
                for k in range(NKC):
                    v.wait_ge(ACs.h, b_ac + 4 + k)
                    v.tensor_tensor(wtmp[0:64, 0:128], wtmp[0:64, 0:128],
                                    dtS[0:64, k * 128:(k + 1) * 128],
                                    OP.mult).then_inc(DV.h, 1)
                    if S5 >= 4 and k >= 1:
                        kk = k - 1
                        v.wait_ge(PE.h, b_pe + kk + 1)
                        pb = pt4 if kk % 2 == 0 else pt5
                        v.tensor_copy(dtwT[:, kk * 64:(kk + 1) * 64], pb[:, 0:64])
                        v.engine_nop().then_inc(EV.h, 1)
                if S5 >= 4:
                    kk = NKC - 1
                    v.wait_ge(PE.h, b_pe + kk + 1)
                    pb = pt4 if kk % 2 == 0 else pt5
                    v.tensor_copy(dtwT[:, kk * 64:(kk + 1) * 64], pb[:, 0:64])
                    v.engine_nop().then_inc(EV.h, 1)
                else:
                    v.engine_nop().then_inc(EV.h, NKC)
                if S5 >= 5:
                    for k in range(NKC):
                        v.wait_ge(PE.h, b_pe + NKC + k + 1)
                        pb = pt4 if k % 2 == 0 else pt5
                        v.tensor_copy(lndtcT[:, k * 64:(k + 1) * 64], pb[:, 0:64])
                        v.engine_nop().then_inc(EV.h, 1)
                else:
                    v.engine_nop().then_inc(EV.h, NKC)
                if S5 >= 6:
                    for h in range(NH):
                        v.wait_ge(PE.h, b_pe + 2 * NKC + h + 1)
                        pb = pt4 if h % 2 == 0 else pt5
                        v.tensor_copy(AP(erepT, h, [[NKC * NH, 128], [64, 16]]),
                                      pb[:, 0:16])
                        v.engine_nop().then_inc(EV.h, 1)
                else:
                    v.engine_nop().then_inc(EV.h, NH)
                DV.v += NKC
                EV.v += 2 * NKC + NH

            @blk.tensor
            def _(t):
                if S5 < 4:
                    return
                for k in range(NKC):
                    t.wait_ge(DV.h, b_dv + 4 + k)
                    if k >= 2:
                        t.wait_ge(EV.h, b_ev + k - 1)
                    pb = pt4 if k % 2 == 0 else pt5
                    t.transpose(pb[:, 0:64], wtmp[0:64, 0:128],
                                i64fS[:, :]).then_inc(PE.h, 1)
                if S5 >= 5:
                    for k in range(NKC):
                        t.wait_ge(EV.h, b_ev + NKC + k - 1)
                        pb = pt4 if k % 2 == 0 else pt5
                        t.transpose(pb[:, 0:64], lnSb[0:64, k * 128:(k + 1) * 128],
                                    i64fS[:, :]).then_inc(PE.h, 1)
                if S5 >= 6:
                    for h in range(NH):
                        t.wait_ge(EV.h, b_ev + 2 * NKC + h - 1)
                        t.wait_ge(ACs.h, b_ac + 3)
                        pb = pt4 if h % 2 == 0 else pt5
                        t.matmul(pb[:, 0:16],
                                 AP(i64dS, 2 * h, [[128, 64], [0, 128]]),
                                 eS[0:64, 0:16], start=True, stop=True,
                                 skip_group_check=True).then_inc(PE.h, 1)
                PE.v += 2 * NKC + NH

            @blk.sync
            def _(s):
                s.wait_ge(EV.h, b_ev + 2 * NKC + NH)
                s.wait_ge(DV.h, b_dv + 3)
                s.dma_start(out=cspl, in_=cS[0:64, 0:T]).then_inc(ST.h, 16)
                s.dma_start(out=tspl[:, 0:1024], in_=dtwT[:, :]).then_inc(ST.h, 16)
                s.dma_start(out=tspl[:, 1024:2048], in_=erepT[:, :]).then_inc(ST.h, 16)
                s.dma_start(out=tspl[:, 2048:3072], in_=lndtcT[:, :]).then_inc(ST.h, 16)
                ST.v += 64
                s.wait_ge(ST.h, ST.v)

        if "scal" in debug:
            emit_dbg_copy("scal", [(dbg["d_c"], cspl), (dbg["d_tsp"], tspl),
                                   (dbg["d_dt"], dtS[:, :])])
        if stop <= 5: return fin()

        with (
            nc.psum_tensor("pd5", [128, 512], F32) as pd5,
            nc.psum_tensor("pd5b", [128, 512], F32) as pd5b,
            nc.psum_tensor("ptb", [128, 128], BF16) as ptb,
            nc.Block("pass1") as blk,
        ):
            b_pe, b_dv, b_ev, b_pl, b_st, b_cc = PE.v, DV.v, EV.v, PL.v, ST.v, CCs.v
            b_ldc = LDC.v
            b_ld = [LD.v, LDB.v]

            @blk.sync
            def _(s):
                s.dma_start(out=sBT[:, :], in_=xcv[DSSM:DSSM + 128, 0:T]).then_inc(LDC.h, 16)
                s.dma_start(out=sCT[:, :], in_=xcv[DSSM + 128:DSSM + 256, 0:T]).then_inc(LDC.h, 16)
                for k in range(NKC):
                    if k >= 2:
                        s.wait_ge(PL.h, b_pl + (k - 1))
                    s.dma_start_transpose(a_xt[k % 2][:, :],
                                          xcv[0:DSSM, k * 128:(k + 1) * 128]
                                          ).then_inc(LDp[k % 2].h, 16)
                LDC.v += 32
                LD.v += 16 * (NKC // 2)
                LDB.v += 16 * (NKC // 2)

            @blk.tensor
            def _(t):
                t.wait_ge(LDC.h, b_ldc + 32)
                for k in range(NKC):
                    if k >= 2:
                        t.wait_ge(EV.h, b_ev + k - 1)
                    t.transpose(ptb[:, :], sBT[:, k * 128:(k + 1) * 128],
                                i128S[:, :]).then_inc(PE.h, 1)
                    t.wait_ge(EV.h, b_ev + k + 1)
                PE.v += NKC

            @blk.vector
            def _(v):
                v.memset(harena[:, :], 0.0)
                v.engine_nop().then_inc(DV.h, 1)
                for k in range(NKC):
                    v.wait_ge(PE.h, b_pe + k + 1)
                    v.tensor_copy(sBtok[:, k * 128:(k + 1) * 128], ptb[:, :])
                    v.engine_nop().then_inc(EV.h, 1)
                EV.v += NKC
                DV.v += 1
            b_pe2 = PE.v
            b_dv2 = DV.v
            b_pl2 = PL.v

            @blk.gpsimd
            def _(g):
                for k in range(NKC):
                    g.wait_ge(LDp[k % 2].h, b_ld[k % 2] + 16 * (k // 2 + 1))
                    if k >= 1:
                        g.wait_ge(PE.h, b_pe2 + 16 * k)
                    g.tensor_tensor(a_xsw[:, :], a_xt[k % 2][:, :],
                                    AP(dtwT, k * 64, [[NKC * NH, 128], [1, 64], [0, 128]]),
                                    OP.mult).then_inc(PL.h, 1)
                PL.v += NKC

            @blk.tensor
            def _(t):
                t.wait_ge(EV.h, b_ev + NKC)
                t.wait_ge(DV.h, b_dv2)
                for k in range(NKC):
                    t.wait_ge(PL.h, b_pl2 + k + 1)
                    for g in range(16):
                        bank = pd5 if g % 2 == 0 else pd5b
                        L = k * 16 + g
                        if L >= 2:
                            t.wait_ge(DV.h, b_dv2 + L - 1)
                        inst = None
                        for j in range(4):
                            h = 4 * g + j
                            inst = t.matmul(bank[:, j * 128:(j + 1) * 128],
                                            sBtok[:, k * 128:(k + 1) * 128],
                                            a_xsw[:, h * 128:(h + 1) * 128],
                                            start=True, stop=True,
                                            skip_group_check=True)
                        inst.then_inc(PE.h, 1)
                PE.v += NKC * 16

            @blk.vector
            def _(v):
                for k in range(NKC):
                    for g in range(16):
                        bank = pd5 if g % 2 == 0 else pd5b
                        v.wait_ge(PE.h, b_pe2 + k * 16 + g + 1)
                        inst = None
                        for j in range(4):
                            h = 4 * g + j
                            inst = v.scalar_tensor_tensor(
                                harena[:, h * 128:(h + 1) * 128],
                                harena[:, h * 128:(h + 1) * 128],
                                erepT[:, h + 64 * k:h + 64 * k + 1],
                                bank[:, j * 128:(j + 1) * 128],
                                OP.mult, OP.add)
                        inst.then_inc(DV.h, 1)
                DV.v += NKC * 16

            @blk.scalar
            def _(sc):
                sc.wait_ge(DV.h, b_dv2 + NKC * 16)
                sc.dma_start(out=stown, in_=harena[:, :]).then_inc(ST.h, 16)
                sc.dma_start(out=btspl, in_=sBtok[:, :]).then_inc(ST.h, 16)
                ST.v += 32

            if mode == "full":
                @blk.gpsimd
                def _(g):
                    g.wait_ge(ST.h, b_st + 32)
                    import os as _os
                    if _os.environ.get("K_NO_CC"):
                        g.dma_start(out=stg[0:128, :], in_=stown).then_inc(CCs.h, 16)
                    else:
                        g.collective_compute(
                            "AllGather", OP.bypass,
                            replica_groups=[[0, 1], [2, 3], [4, 5], [6, 7]],
                            ins=[stown], outs=[stg],
                        ).then_inc(CCs.h, 16)
                    CCs.v += 16

                @blk.sync
                def _(s):
                    s.wait_ge(CCs.h, b_cc + 16)
                    s.dma_start(out=a_xt[0][:, :], in_=stg[0:128, :]).then_inc(LD.h, 16)
                    LD.v += 16

                @blk.vector
                def _(v):
                    v.wait_ge(LD.h, LD.v)
                    v.tensor_scalar(harena[:, :], a_xt[0][:, :], rmaskS[:, :], None,
                                    OP.mult).then_inc(DV.h, 1)
                    DV.v += 1
            else:
                @blk.sync
                def _(s):
                    s.wait_ge(ST.h, b_st + 32)

        if "pass1" in debug:
            emit_dbg_copy("p1", [(dbg["d_h"], stown), (dbg["d_hin"], stg[0:128, :])])
        if stop <= 6: return fin()
    if mode == "A": return fin()

    def esc_pos(g):
        return 2 if g == 0 else 2 * g + 1

    def ygp_pos(g):
        return 2 * g + 4

    def dec_pos(g):
        return 2 * g + 4

    def emit_ssd_half(hf):
        kset = list(range(hf * 8, hf * 8 + 8))
        NG = 16
        with (
            nc.psum_tensor("pb1a%d" % hf, [128, 512], F32) as pb1a,
            nc.psum_tensor("pb1b%d" % hf, [128, 512], F32) as pb1b,
            nc.psum_tensor("pya%d" % hf, [128, 512], F32) as pya,
            nc.psum_tensor("pyb%d" % hf, [128, 512], F32) as pyb,
            nc.psum_tensor("pca%d" % hf, [128, 512], F32) as pca,
            nc.psum_tensor("pcb%d" % hf, [128, 512], F32) as pcb,
            nc.psum_tensor("pdd%d" % hf, [128, 512], F32) as pdd,
            nc.psum_tensor("pg0%d" % hf, [128, 128], F32) as pg0,
            nc.Block("ssd%d" % hf) as blk,
        ):
            pb1 = [pb1a, pb1b]
            pyy = [pya, pyb]
            pcc = [pca, pcb]
            b_pe, b_dv = PE.v, DV.v
            b_ac, b_pl, b_ev, b_ew = ACs.v, PL.v, EV.v, EW.v
            b_ldc, b_ldz, b_sto = LDC.v, LDZ.v, STO.v
            b_ld = [LD.v, LDB.v]
            b_fl = [FLA.v, FLB.v]

            @blk.sync
            def _(s):
                s.dma_start(out=dtwT[:, :], in_=tspl[:, 0:1024]).then_inc(LDC.h, 16)
                s.dma_start(out=erepT[:, :], in_=tspl[:, 1024:2048]).then_inc(LDC.h, 16)
                s.dma_start(out=lndtcT[:, :], in_=tspl[:, 2048:3072]).then_inc(LDC.h, 16)
                s.dma_start(out=sBT[:, :], in_=xcv[DSSM:DSSM + 128, 0:T]).then_inc(LDC.h, 16)
                s.dma_start(out=sCT[:, :], in_=xcv[DSSM + 128:DSSM + 256, 0:T]).then_inc(LDC.h, 16)
                s.dma_start(out=sBtok[:, :], in_=btspl).then_inc(LDC.h, 16)
                s.dma_start(out=a_di[:, :], in_=drepI).then_inc(LDC.h, 16)
                for ki, k in enumerate(kset):
                    if ki >= 2:
                        s.wait_ge(EW.h, b_ew + (ki - 2) * 17 + 16)
                    s.dma_start_transpose(a_xt[ki % 2][:, :],
                                          xcv[0:DSSM, k * 128:(k + 1) * 128]
                                          ).then_inc(LDp[ki % 2].h, 16)
                    if ki >= 1:
                        s.wait_ge(ACs.h, b_ac + (ki - 1) * 36 + 1)
                    s.dma_start(out=a_z[:, :],
                                in_=AP(zsp.tensor, k * 128,
                                       [[T, 128], [128 * T, CC], [1, 128]])
                                ).then_inc(LDZ.h, 16)
                    for q in range(4):
                        Q = ki * 4 + q
                        if Q >= 2:
                            kk, qq = divmod(Q - 2, 4)
                            s.wait_ge(PE.h, b_pe + kk * 17 + 2 + 4 * qq + 3)
                        s.dma_start(out=flc[q % 2][0:1, 0:2048],
                                    in_=AP(cspl.tensor, 16 * q * T + k * 128,
                                           [[T, 1], [T, 16], [1, 128]])
                                    ).then_inc(FLp[q % 2].h, 16)
                LDC.v += 16 * 7
                LD.v += 16 * ((len(kset) + 1) // 2)
                LDB.v += 16 * (len(kset) // 2)
                LDZ.v += 16 * len(kset)
                FLA.v += 16 * len(kset) * 2
                FLB.v += 16 * len(kset) * 2

            @blk.tensor
            def _(t):
                t.wait_ge(LDC.h, b_ldc + 16 * 7)
                for ki, k in enumerate(kset):
                    if ki >= 1:
                        t.wait_ge(EV.h, b_ev + (ki - 1) * 34 + 33)
                        t.wait_ge(DV.h, b_dv + ki * 16 + 1) if False else None
                    t.matmul(pg0[:, :], sBT[:, k * 128:(k + 1) * 128],
                             sCT[:, k * 128:(k + 1) * 128],
                             start=True, stop=True).then_inc(PE.h, 1)

                    def bcast(g):
                        q = g // 4
                        t.wait_ge(FLp[q % 2].h, b_fl[q % 2] + 16 * (ki * 2 + q // 2 + 1))
                        if g >= 2:
                            t.wait_ge(ACs.h, b_ac + ki * 36 + esc_pos(g - 2))
                            t.wait_ge(DV.h, b_dv + ki * 16 + (g - 2) + 1)
                        elif ki >= 1:
                            t.wait_ge(ACs.h, b_ac + (ki - 1) * 36 + esc_pos(g + 14))
                            t.wait_ge(DV.h, b_dv + (ki - 1) * 16 + g + 14 + 1)
                        inst = None
                        for j in range(4):
                            inst = t.matmul(
                                pb1[g % 2][:, j * 128:(j + 1) * 128], onesFrS[:, :],
                                flc[(g // 4) % 2][0:1, (g % 4) * 512 + j * 128:
                                                  (g % 4) * 512 + (j + 1) * 128],
                                start=True, stop=True, skip_group_check=True)
                        inst.then_inc(PE.h, 1)

                    def head(g):
                        t.wait_ge(PL.h, b_pl + ki * 34 + (2 if g == 0 else 2 * g + 1))
                        if g >= 2:
                            t.wait_ge(EV.h, b_ev + ki * 34 + 2 * (g - 2) + 1)
                        elif ki >= 1:
                            t.wait_ge(EV.h, b_ev + (ki - 1) * 34 + 2 * (g + 14) + 1)
                        if g >= 1:
                            t.wait_ge(EV.h, b_ev + ki * 34 + 2 * (g - 1) + 2)
                        elif ki >= 1:
                            t.wait_ge(EV.h, b_ev + (ki - 1) * 34 + 32)
                        inst = None
                        for j in range(4):
                            h = 4 * g + j
                            hs = slice(h * 128, (h + 1) * 128)
                            js = slice(j * 128, (j + 1) * 128)
                            t.matmul(pyy[g % 2][:, js], a_xt[ki % 2][:, hs],
                                     g_gpp[:, (g % 2) * 512 + j * 128:
                                           (g % 2) * 512 + (j + 1) * 128],
                                     start=True, stop=False, skip_group_check=True)
                            t.matmul(pyy[g % 2][:, js], a_xt[ki % 2][:, hs],
                                     a_di[:, hs],
                                     start=False, stop=True, skip_group_check=True)
                            t.matmul(pcc[g % 2][:, js], harena[:, hs],
                                     sCT[:, k * 128:(k + 1) * 128],
                                     start=True, stop=True, skip_group_check=True)
                            inst = t.matmul(pdd[:, js], sBtok[:, k * 128:(k + 1) * 128],
                                            a_xsw[:, hs],
                                            start=True, stop=True,
                                            skip_group_check=True)
                        inst.then_inc(EW.h, 1)

                    bcast(0)
                    for g in range(1, NG):
                        bcast(g)
                        head(g - 1)
                    head(NG - 1)
                    t.wait_ge(ACs.h, b_ac + ki * 36 + 35)
                    t.wait_ge(DV.h, b_dv + ki * 16 + 1)
                    inst = None
                    for cc2 in range(CC):
                        inst = t.matmul(pg0[0:1, 0:128], onesBS[:, :],
                                        a_sq[:, cc2 * 128:(cc2 + 1) * 128],
                                        start=(cc2 == 0), stop=(cc2 == CC - 1),
                                        skip_group_check=True)
                    inst.then_inc(EW.h, 1)
                PE.v += len(kset) * 17
                EW.v += len(kset) * 17

            @blk.vector
            def _(v):
                v.wait_ge(LDC.h, b_ldc + 16 * 7)
                for ki, k in enumerate(kset):
                    def dmin(g):
                        v.wait_ge(PE.h, b_pe + ki * 17 + 2 + g)
                        if g == 0:
                            if ki >= 1:
                                v.wait_ge(PL.h, b_pl + (ki - 1) * 34 + 31)
                            vs(v, v.tensor_tensor(g_g0m[:, :], pg0[:, :], maskS[:, :],
                                            OP.mult))
                        if g >= 2:
                            v.wait_ge(ACs.h, b_ac + ki * 36 + dec_pos(g - 2))
                        elif ki >= 1:
                            v.wait_ge(ACs.h, b_ac + (ki - 1) * 36 + dec_pos(g + 14))
                        inst = None
                        for j in range(4):
                            h = 4 * g + j
                            inst = v.tensor_scalar(
                                g_dmin[:, (g % 2) * 512 + j * 128:
                                       (g % 2) * 512 + (j + 1) * 128],
                                pb1[g % 2][:, j * 128:(j + 1) * 128],
                                lndtcT[:, k * 64 + h:k * 64 + h + 1], 30.0,
                                OP.add, OP.min)
                        inst.then_inc(DV.h, 1)

                    def comb(g):
                        v.wait_ge(EW.h, b_ew + ki * 17 + g + 1)
                        v.wait_ge(ACs.h, b_ac + ki * 36 + esc_pos(g))
                        vs(v, v.tensor_tensor(g_t1[:, :], pcc[g % 2][:, 0:512],
                                        g_esc[:, (g % 2) * 512:(g % 2) * 512 + 512],
                                        OP.mult))
                        if g >= 1:
                            v.wait_ge(PL.h, b_pl + ki * 34 + ygp_pos(g - 1))
                        elif ki >= 1:
                            v.wait_ge(PL.h, b_pl + (ki - 1) * 34 + 34)
                        v.tensor_tensor(g_ysum[:, :], g_t1[:, :],
                                        pyy[g % 2][:, 0:512],
                                        OP.add).then_inc(EV.h, 1)
                        inst = None
                        for j in range(4):
                            h = 4 * g + j
                            hs = slice(h * 128, (h + 1) * 128)
                            inst = v.scalar_tensor_tensor(
                                harena[:, hs], harena[:, hs],
                                erepT[:, h + 64 * k:h + 64 * k + 1],
                                pdd[:, j * 128:(j + 1) * 128],
                                OP.mult, OP.add)
                        inst.then_inc(EV.h, 1)

                    dmin(0)
                    dmin(1)
                    for g in range(NG - 1):
                        if g + 2 < NG:
                            dmin(g + 2)
                        comb(g)
                    comb(NG - 1)
                    v.wait_ge(EW.h, b_ew + ki * 17 + 17)
                    vs(v, v.tensor_copy(sqsR[0:1, 0:128], pg0[0:1, 0:128]))
                    vs(v, v.tensor_scalar_mul(sqsR[0:1, 0:128], sqsR[0:1, 0:128],
                                        1.0 / DSSM))
                    v.engine_nop().then_inc(EV.h, 1)
                    v.wait_ge(ACs.h, b_ac + ki * 36 + 36)
                    vs(v, v.reciprocal(sqsR[0:1, 0:128], sqsR[0:1, 0:128]))
                    vs(v, v.tensor_copy(rstdH[0:1, ki * 128:(ki + 1) * 128],
                                  sqsR[0:1, 0:128]))
                    v.engine_nop().then_inc(EV.h, 1)
                DV.v += len(kset) * 16
                EV.v += len(kset) * 34

            @blk.scalar
            def _(sc):
                sc.wait_ge(LDC.h, b_ldc + 16 * 7)
                for ki, k in enumerate(kset):
                    sc.wait_ge(LDZ.h, b_ldz + 16 * (ki + 1))
                    if ki >= 1:
                        sc.wait_ge(PL.h, b_pl + (ki - 1) * 34 + 34)
                    sc.activation(a_szn[:, :], a_z[:, :], AF.Silu).then_inc(ACs.h, 1)

                    def esc(g):
                        sc.wait_ge(PE.h, b_pe + ki * 17 + 2 + g)
                        if g >= 2:
                            sc.wait_ge(EV.h, b_ev + ki * 34 + 2 * (g - 2) + 1)
                        elif ki >= 1:
                            sc.wait_ge(EV.h, b_ev + (ki - 1) * 34 + 2 * (g + 14) + 1)
                        sc.activation(g_esc[:, (g % 2) * 512:(g % 2) * 512 + 512],
                                      pb1[g % 2][:, 0:512], AF.Exp).then_inc(ACs.h, 1)

                    def dec(g):
                        sc.wait_ge(DV.h, b_dv + ki * 16 + g + 1)
                        if g >= 2:
                            gp = g - 2
                            sc.wait_ge(PL.h, b_pl + ki * 34 +
                                       (2 if gp == 0 else 2 * gp + 1))
                        elif ki >= 1:
                            sc.wait_ge(PL.h, b_pl + (ki - 1) * 34 + 2 * (g + 14) + 1)
                        sc.activation(g_dec[:, (g % 2) * 512:(g % 2) * 512 + 512],
                                      g_dmin[:, (g % 2) * 512:(g % 2) * 512 + 512],
                                      AF.Exp).then_inc(ACs.h,
                                                       2 if g == NG - 1 else 1)

                    esc(0)
                    for g in range(1, NG):
                        esc(g)
                        dec(g - 1)
                    dec(NG - 1)
                    sc.wait_ge(PL.h, b_pl + ki * 34 + 34)
                    sc.activation(a_sq[:, :], a_yg[:, :], AF.Square).then_inc(ACs.h, 1)
                    sc.wait_ge(EV.h, b_ev + ki * 34 + 33)
                    sc.activation(sqsR[0:1, 0:128], sqsR[0:1, 0:128], AF.Sqrt,
                                  bias=epsS[0:1, 0:1]).then_inc(ACs.h, 1)
                    sc.dma_start(out=AP(ygd.tensor, k * 128,
                                        [[T, 128], [128 * T, CC], [1, 128]]),
                                 in_=a_yg[:, :]).then_inc(STO.h, 16)
                ACs.v += len(kset) * 36
                STO.v += 16 * len(kset)

            @blk.gpsimd
            def _(g_):
                g_.wait_ge(LDC.h, b_ldc + 16 * 7)
                for ki, k in enumerate(kset):
                    g_.wait_ge(LDp[ki % 2].h, b_ld[ki % 2] + 16 * (ki // 2 + 1))
                    if ki >= 1:
                        g_.wait_ge(EW.h, b_ew + (ki - 1) * 17 + 17)
                        g_.wait_ge(ACs.h, b_ac + (ki - 1) * 36 + 35)
                    g_.tensor_tensor(a_xsw[:, :], a_xt[ki % 2][:, :],
                                     AP(dtwT, k * 64,
                                        [[NKC * NH, 128], [1, 64], [0, 128]]),
                                     OP.mult).then_inc(PL.h, 1)

                    def gpp(g):
                        g_.wait_ge(ACs.h, b_ac + ki * 36 + dec_pos(g))
                        if g >= 2:
                            g_.wait_ge(EW.h, b_ew + ki * 17 + (g - 2) + 1)
                        elif ki >= 1:
                            g_.wait_ge(EW.h, b_ew + (ki - 1) * 17 + g + 14 + 1)
                        g_.tensor_tensor(g_gpp[:, (g % 2) * 512:(g % 2) * 512 + 512],
                                         g_dec[:, (g % 2) * 512:(g % 2) * 512 + 512],
                                         AP(g_g0m, 0, [[128, 128], [0, 4], [1, 128]]),
                                         OP.mult).then_inc(PL.h, 1)

                    def yg(g):
                        g_.wait_ge(EV.h, b_ev + ki * 34 + 2 * g + 1)
                        g_.wait_ge(ACs.h, b_ac + ki * 36 + 1)
                        if g == 0:
                            g_.wait_ge(STO.h, b_sto + 16 * ki)
                        g_.tensor_tensor(a_yg[:, g * 512:(g + 1) * 512],
                                         g_ysum[:, :],
                                         a_szn[:, g * 512:(g + 1) * 512],
                                         OP.mult).then_inc(PL.h,
                                                           2 if g == NG - 1 else 1)

                    gpp(0)
                    for g in range(1, NG):
                        gpp(g)
                        yg(g - 1)
                    yg(NG - 1)
                PL.v += len(kset) * 34

            @blk.sync
            def _(s):
                s.wait_ge(STO.h, b_sto + 16 * len(kset))
                s.wait_ge(EV.h, b_ev + len(kset) * 34)

    def emit_wout_half(hf):
        with (
            nc.psum_tensor("pwo0%d" % hf, [128, 512], F32) as pwo0,
            nc.psum_tensor("pwo1%d" % hf, [128, 512], F32) as pwo1,
            nc.psum_tensor("pwo2%d" % hf, [128, 512], F32) as pwo2,
            nc.Block("wout%d" % hf) as blk,
        ):
            pws = [pwo0, pwo1, pwo2]
            b_pe, b_dv = PE.v, DV.v
            b_ldc = LDC.v
            b_ld = [LD.v, LDB.v]
            b_lg = [LG.v, LGB.v]
            b_st = [ST.v, STB.v]

            @blk.sync
            def _(s):
                s.dma_start(out=a_yh[:, :],
                            in_=AP(ygd.tensor, hf * 1024,
                                   [[T, 128], [128 * T, CC], [1, 1024]])
                            ).then_inc(LDC.h, 16)
                LDC.v += 16
                for ot in range(DC):
                    if ot >= 2:
                        s.wait_ge(DV.h, b_dv + 2 + 2 * (ot - 1))
                    s.dma_start(out=wo_x[ot % 2][:, :],
                                in_=x1T[ot * 128:(ot + 1) * 128,
                                        TH + hf * 1024:TH + hf * 1024 + 1024]
                                ).then_inc(LDp[ot % 2].h, 16)
                LD.v += 16 * (DC // 2)
                LDB.v += 16 * (DC // 2)

            @blk.gpsimd
            def _(g):
                for ot in range(DC):
                    if ot >= 2:
                        g.wait_ge(PE.h, b_pe + 1 + 2 * (ot - 1))
                    g.dma_start(out=wo_b[ot % 2][:, :],
                                in_=AP(wout.tensor, ot * 128,
                                       [[D, 128], [128 * D, CC], [1, 128]])
                                ).then_inc(LGp[ot % 2].h, 16)
                LG.v += 16 * (DC // 2)
                LGB.v += 16 * (DC // 2)

            @blk.vector
            def _(v):
                v.wait_ge(LDC.h, b_ldc + 16)
                for cc2 in range(CC):
                    v.tensor_scalar(a_yh[:, cc2 * 1024:(cc2 + 1) * 1024],
                                    a_yh[:, cc2 * 1024:(cc2 + 1) * 1024],
                                    nwS[:, cc2:cc2 + 1], None, OP.mult)
                v.engine_nop().then_inc(DV.h, 1)
                v.wait_ge(PE.h, b_pe + 1)
                v.tensor_copy(rbB[0][:, :], pwo0[:, 0:512])
                v.tensor_copy(rbB[1][:, :], pwo1[:, 0:512])
                v.engine_nop().then_inc(DV.h, 1)
                for ot in range(DC):
                    v.wait_ge(LDp[ot % 2].h, b_ld[ot % 2] + 16 * (ot // 2 + 1))
                    for sl in range(2):
                        idx = ot * 2 + sl
                        v.wait_ge(PE.h, b_pe + 1 + idx + 1)
                        if idx >= 2:
                            v.wait_ge(STp[idx % 2].h, b_st[idx % 2] + 16 * (idx // 2))
                        vs(v, v.tensor_tensor(wo_e[idx % 2][:, :], pws[idx % 3][:, 0:512],
                                        rbB[sl][:, :], OP.mult))
                        v.tensor_tensor(wo_e[idx % 2][:, :], wo_e[idx % 2][:, :],
                                        wo_x[ot % 2][:, sl * 512:(sl + 1) * 512],
                                        OP.add).then_inc(DV.h, 1)
                DV.v += 2 + DC * 2

            @blk.tensor
            def _(t):
                t.matmul(pwo0[:, 0:512], onesFrS[:, :], rstdH[0:1, 0:512],
                         start=True, stop=True, skip_group_check=True)
                t.matmul(pwo1[:, 0:512], onesFrS[:, :], rstdH[0:1, 512:1024],
                         start=True, stop=True,
                         skip_group_check=True).then_inc(PE.h, 1)
                for ot in range(DC):
                    t.wait_ge(LGp[ot % 2].h, b_lg[ot % 2] + 16 * (ot // 2 + 1))
                    t.wait_ge(DV.h, b_dv + 1)
                    for sl in range(2):
                        idx = ot * 2 + sl
                        if idx < 2:
                            t.wait_ge(DV.h, b_dv + 2)
                        else:
                            t.wait_ge(DV.h, b_dv + 2 + idx - 2)
                        inst = None
                        for cc2 in range(CC):
                            inst = t.matmul(
                                pws[idx % 3][:, 0:512],
                                wo_b[ot % 2][:, cc2 * 128:(cc2 + 1) * 128],
                                a_yh[:, cc2 * 1024 + sl * 512:
                                     cc2 * 1024 + sl * 512 + 512],
                                start=(cc2 == 0), stop=(cc2 == CC - 1))
                        inst.then_inc(PE.h, 1)
                PE.v += 1 + DC * 2

            @blk.scalar
            def _(sc):
                for ot in range(DC):
                    for sl in range(2):
                        idx = ot * 2 + sl
                        sc.wait_ge(DV.h, b_dv + 2 + idx + 1)
                        sc.dma_start(out=x2T[ot * 128:(ot + 1) * 128,
                                             hf * 1024 + sl * 512:
                                             hf * 1024 + sl * 512 + 512],
                                     in_=wo_e[idx % 2][:, :]).then_inc(STp[idx % 2].h, 16)
                ST.v += 16 * DC
                STB.v += 16 * DC

            @blk.sync
            def _(s):
                s.wait_ge(ST.h, b_st[0] + 16 * DC)
                s.wait_ge(STB.h, b_st[1] + 16 * DC)

    for hf in range(2):
        emit_ssd_half(hf)
        if hf == 0 and "hp2" in debug:
            emit_dbg_copy("hp2", [(dbg["d_hp2"], harena[:, :])])
        if hf == 1 and "ssdint" in debug:
            emit_dbg_copy("ssdint", [
                (dbg["d_g0m"], g_g0m[:, :]), (dbg["d_dec"], g_dec[:, :]),
                (dbg["d_esc"], g_esc[:, :]), (dbg["d_gpp"], g_gpp[:, :]),
                (dbg["d_dmin"], g_dmin[:, :]), (dbg["d_ysum"], g_ysum[:, :]),
                (dbg["d_axt"], a_xt[1][:, :]), (dbg["d_aszn"], a_szn[:, :]),
                (dbg["d_ayg2"], a_yg[:, :]), (dbg["d_axsw"], a_xsw[:, :]),
                (dbg["d_adi"], a_di[:, :]), (dbg["d_az"], a_z[:, :]),
                (dbg["d_sbt"], sBT[:, :]), (dbg["d_sct"], sCT[:, :]),
                (dbg["d_sbtok"], sBtok[:, :]), (dbg["d_dtw"], dtwT[:, :]),
                (dbg["d_erep"], erepT[:, :]), (dbg["d_lndtc"], lndtcT[:, :])])
        if stop <= 7 + 2 * hf: return fin()
        if "yg" in debug and hf == 1:
            emit_dbg_copy("yg", [(dbg["d_yg"], ygd)])
            with nc.Block("dbgr") as blk:
                @blk.sync
                def _(s):
                    s.dma_start(out=dbg["d_rstd"][0:1, T:T + 1024],
                                in_=rstdH[0:1, :]).then_inc(ST.h, 16)
                    ST.v += 16
                    s.wait_ge(ST.h, ST.v)
        emit_wout_half(hf)
        if stop <= 8 + 2 * hf: return fin()

    if "x2" in debug:
        emit_dbg_copy("x2", [(dbg["d_x2"], x2T)])

    emit_ln("3", x2T, T, TSL_T, T)
    if stop <= 11: return fin()
    emit_ffn("2", T, TSL_T, x2T, xoutT, T)

    stk.close()
    return nc, dbg


def _prep_inputs(inputs):
    bf = ml_dtypes.bfloat16
    f32 = np.float32
    x = np.asarray(inputs["L0_attn_hidden_states"], f32)
    W_in = np.ascontiguousarray(np.asarray(inputs["L0_attn_W_in"], f32))
    conv_w = np.asarray(inputs["L0_attn_conv_w"], f32)
    dt_bias = np.asarray(inputs["L0_attn_dt_bias"], f32)
    A_log = np.asarray(inputs["L0_attn_A_log"], f32)
    D_param = np.asarray(inputs["L0_attn_D_param"], f32)
    norm_w = np.asarray(inputs["L0_attn_norm_weight"], f32)
    W_out = np.ascontiguousarray(np.asarray(inputs["L0_attn_W_out"], f32))
    fw = np.asarray(inputs["L0_ffn_weight"], f32)
    fb = np.asarray(inputs["L0_ffn_bias"], f32)
    lA = np.asarray(inputs["L0_ffn_lora_A"], f32)
    lB = np.asarray(inputs["L0_ffn_lora_B"], f32)
    lsi = np.asarray(inputs["L0_ffn_lora_start_indices"]).astype(np.int32)

    shared = {
        "win": W_in,
        "wout": W_out,
        "wffnT": np.ascontiguousarray(fw.T),
        "biasT": np.ascontiguousarray(fb.reshape(DC, 128).T),
        "nwTi": np.ascontiguousarray(norm_w.reshape(NH, 128).T),
        "dtbI": np.ascontiguousarray(dt_bias.reshape(NH, 1)),
        "alogI": np.ascontiguousarray(A_log.reshape(NH, 1)),
        "convwI": np.ascontiguousarray(
            conv_w.reshape(XC, 128, 4).transpose(1, 0, 2).reshape(128, XC * 4)),
        "lATi": np.ascontiguousarray(lA.reshape(48, D).T),
        "lBTi": np.ascontiguousarray(lB.transpose(0, 2, 1).reshape(48, D)),
        "lsiI": np.ascontiguousarray(lsi.reshape(4, 1)),
        "cI128": np.eye(128, dtype=bf),
        "cI128f": np.eye(128, dtype=f32),
        "cI64f": np.eye(64, dtype=f32),
        "cI64d": np.kron(np.eye(64, dtype=f32), np.array([[1, 0]], f32)),
        "cI4f": np.eye(4, dtype=f32),
        "cMask": np.triu(np.ones((128, 128), f32)),
        "cOnesB": np.ones((128, 1), bf),
        "cOnesFr": np.ones((1, 128), f32),
        "cREP": np.kron(np.eye(3, dtype=f32), np.ones((1, 16), f32)).astype(f32),
        "cIota3": np.arange(3, dtype=f32).reshape(1, 3),
    }
    drep = np.zeros((128, DSSM), f32)
    for h in range(NH):
        idx = np.arange(128)
        drep[idx, h * 128 + idx] = D_param[h]
    shared["drepI"] = drep.astype(bf)

    in_maps = []
    for c in range(8):
        b, half = c // 2, c % 2
        s0 = half * T
        if half == 0:
            halo = np.zeros((TH, D), f32)
        else:
            halo = x[b, s0 - TH:s0, :]
        xs = np.concatenate([halo, x[b, s0:s0 + T, :]], axis=0)
        m = dict(shared)
        m["x0T"] = np.ascontiguousarray(xs.T)
        m["metaI"] = np.full((4, 1), float(b), f32)
        m["rmaskI"] = np.full((128, 1), float(half), f32)
        in_maps.append(m)
    return in_maps


def kernel(**inputs):
    in_maps = _prep_inputs(inputs)
    ncA, _ = build_program(mode="A")
    resA = run_bass_kernel_spmd(ncA, in_maps, core_ids=list(range(8)))
    bnames = ["x1T", "zsp", "xcv", "cspl", "tspl", "btspl"]
    in_maps_b = []
    for c in range(8):
        m = dict(in_maps[c])
        for k in bnames:
            m[k] = resA.results[c][k]
        if c % 2 == 1:
            m["hinit"] = resA.results[c - 1]["stown"]
        else:
            m["hinit"] = np.zeros((128, DSSM), ml_dtypes.bfloat16)
        in_maps_b.append(m)
    ncB, _ = build_program(mode="B")
    resB = run_bass_kernel_spmd(ncB, in_maps_b, core_ids=list(range(8)))
    out = np.empty((4, 2 * T, D), np.float32)
    for c in range(8):
        b, half = c // 2, c % 2
        out[b, half * T:(half + 1) * T, :] = resB.results[c]["xoutT"].T
    return out
